# version 22
# baseline (speedup 1.0000x reference)
import numpy as np
from contextlib import ExitStack

import concourse.bass as bass
import concourse.mybir as mybir
from concourse.bass_utils import run_bass_kernel_spmd

F32 = mybir.dt.float32
BF16 = mybir.dt.bfloat16
AF = mybir.ActivationFunctionType
ALU = mybir.AluOpType

P = 128
D = 1024
KC = D // P
DH = 4096
T = 2048
SEQ_PER_CORE = 4
TM = 1024
TS = 512
NSUB = TM // TS
NMACRO = SEQ_PER_CORE * T // TM
INV_SQRT2 = 0.7071067811865476
LN_EPS = 1e-5

ENGINES = ("tensor", "vector", "scalar", "gpsimd", "sync")

DEBUG_PHASES = "all"
DEBUG_NMACRO = None


class Prog:

    def __init__(self, nc, ctx):
        self.nc = nc
        self.ops = []
        self.eng_ops = {e: [] for e in ENGINES}
        self.last_write = {}
        self.readers = {}
        self.esem = {e: ctx.enter_context(nc.semaphore(f"se_{e}"))
                     for e in ("tensor", "vector", "scalar", "gpsimd")}
        self.NLANE = 8
        self.qsem = {q: [ctx.enter_context(nc.semaphore(f"sq_{q}{i}"))
                         for i in range(self.NLANE)]
                     for q in ("qs", "qg")}
        self.qcount = {q: 0 for q in ("qs", "qg")}
        self.pending_fence = {e: set() for e in ENGINES}

    def _stream(self, op):
        if op["kind"] == "dma":
            return (op["queue"], op["lane"])
        return op["eng"]

    def _record(self, eng, fn, reads, writes, kind, queue=None):
        opid = len(self.ops)
        op = dict(id=opid, eng=eng, fn=fn, kind=kind, queue=queue,
                  need_tok=False, eidx=len(self.eng_ops[eng]))
        if kind == "dma":
            op["qidx"] = self.qcount[queue]
            op["lane"] = op["qidx"] % self.NLANE
            op["cycle"] = op["qidx"] // self.NLANE
            self.qcount[queue] += 1
        sid = self._stream(op)
        deps = set(self.pending_fence[eng])
        self.pending_fence[eng] = set()
        for r in reads:
            w = self.last_write.get(r)
            if w is not None:
                deps.add(w)
            self.readers.setdefault(r, {})[sid] = opid
        for w in writes:
            lw = self.last_write.get(w)
            if lw is not None:
                deps.add(lw)
            for rd in self.readers.get(w, {}).values():
                if rd != opid:
                    deps.add(rd)
            self.last_write[w] = opid
            self.readers[w] = {}
        op["deps"] = deps
        self.ops.append(op)
        self.eng_ops[eng].append(op)
        return opid

    def comp(self, eng, fn, reads=(), writes=()):
        return self._record(eng, fn, reads, writes, "comp")

    def dma(self, eng, queue, fn, reads=(), writes=()):
        return self._record(eng, fn, reads, writes, "dma", queue=queue)

    def fence(self, keys):
        dep = set()
        for k in keys:
            w = self.last_write.get(k)
            if w is not None:
                dep.add(w)
            dep.update(self.readers.get(k, {}).values())
        for e in ENGINES:
            self.pending_fence[e] |= dep

    def final_wait(self, eng, keys):
        self._record(eng, None, keys, (), "final")

    def emit(self, block):
        for op in self.ops:
            for d in op["deps"]:
                prod = self.ops[d]
                if prod["kind"] == "dma":
                    pass
                elif prod["eng"] == op["eng"] and op["eng"] == "tensor":
                    continue
                else:
                    prod["need_tok"] = True
        for e in ("tensor", "vector", "scalar", "gpsimd"):
            c = 0
            for op in self.eng_ops[e]:
                if op["kind"] == "comp" and op["need_tok"]:
                    c += 1
                op["tok"] = c
        hw = {e: {} for e in ENGINES}
        for op in self.ops:
            waits = {}
            if op["kind"] == "dma" and op["cycle"] >= 1:
                sem = self.qsem[op["queue"]][op["lane"]]
                waits[id(sem)] = (sem, 16 * op["cycle"])
            for d in sorted(op["deps"]):
                prod = self.ops[d]
                if prod["kind"] == "dma":
                    sem = self.qsem[prod["queue"]][prod["lane"]]
                    val = 16 * (prod["cycle"] + 1)
                elif prod["eng"] == op["eng"] and op["eng"] == "tensor":
                    continue
                else:
                    sem, val = self.esem[prod["eng"]], prod["tok"]
                    if val == 0:
                        continue
                k = id(sem)
                waits[k] = (sem, max(val, waits[k][1]) if k in waits else val)
            op["waits"] = []
            for k, (sem, val) in waits.items():
                if hw[op["eng"]].get(k, -1) >= val:
                    continue
                hw[op["eng"]][k] = val
                op["waits"].append((sem, val))

        def body(engine_name):
            def run(eng):
                for op in self.eng_ops[engine_name]:
                    for sem, val in op["waits"]:
                        eng.wait_ge(sem, val)
                    if op["fn"] is None:
                        continue
                    instr = op["fn"](eng)
                    if op["kind"] == "dma":
                        instr.then_inc(self.qsem[op["queue"]][op["lane"]], 16)
                    elif op["need_tok"]:
                        instr.then_inc(self.esem[op["eng"]], 1)
            return run

        block.tensor(body("tensor"))
        block.vector(body("vector"))
        block.scalar(body("scalar"))
        block.gpsimd(body("gpsimd"))
        block.sync(body("sync"))


def overlay_alloc(nc, ctx, arena_name, size_bytes, members):
    arena = ctx.enter_context(
        nc.sbuf_tensor(arena_name, [P, size_bytes], mybir.dt.uint8))
    out = {}
    for (name, shape, dtype, off) in members:
        esz = mybir.dt.size(dtype)
        nbytes = int(np.prod(shape[1:])) * esz
        assert off % 4 == 0 and off + nbytes <= size_bytes, (name, off, nbytes)
        ap = arena[:, off:off + nbytes].bitcast(dtype)
        if len(shape) == 3:
            ap = ap.rearrange("p (a b) -> p a b", b=shape[2])
        else:
            assert len(shape) == 2 and ap.shape[1] == shape[1]
        out[name] = ap
    return out


def build_nc(race_detect=True):
    nc = bass.Bass("TRN2", target_bir_lowering=False,
                   detect_race_conditions=race_detect)

    x_h = nc.dram_tensor("x", [SEQ_PER_CORE, D, T], F32, kind="ExternalInput").ap()
    w1_h = nc.dram_tensor("w1", [D, DH], F32, kind="ExternalInput").ap()
    w2_h = nc.dram_tensor("w2", [DH, D], F32, kind="ExternalInput").ap()
    win_h = nc.dram_tensor("winT", [D, 2 * D], F32, kind="ExternalInput").ap()
    wg_h = nc.dram_tensor("wgT", [D, 2 * D], F32, kind="ExternalInput").ap()
    wo_h = nc.dram_tensor("woT", [D, D], F32, kind="ExternalInput").ap()
    bconv_h = nc.dram_tensor("bconv", [P, KC], F32, kind="ExternalInput").ap()
    fb_h = nc.dram_tensor("fb", [P, KC], F32, kind="ExternalInput").ap()
    bg_h = nc.dram_tensor("bg", [P, 2 * KC], F32, kind="ExternalInput").ap()
    b1_h = nc.dram_tensor("b1", [P, DH // P], F32, kind="ExternalInput").ap()
    b2_h = nc.dram_tensor("b2", [P, KC], F32, kind="ExternalInput").ap()
    wcf_h = nc.dram_tensor("wcf", [P, KC * 4], F32, kind="ExternalInput").ap()
    out_h = nc.dram_tensor("xout", [SEQ_PER_CORE, D, T], F32, kind="ExternalOutput").ap()

    w1_s = nc.dram_tensor("w1_s", [D, DH], BF16).ap()
    w2_s = nc.dram_tensor("w2_s", [DH, D], BF16).ap()
    win_s = nc.dram_tensor("win_s", [D, 2 * D], BF16).ap()
    wg_s = nc.dram_tensor("wg_s", [D, 2 * D], BF16).ap()
    wo_s = nc.dram_tensor("wo_s", [D, D], BF16).ap()

    ctx = ExitStack()
    nc._kernel_ctx = ctx

    sb = lambda name, shape, dt: ctx.enter_context(nc.sbuf_tensor(name, shape, dt))
    KB = 1024
    x_t = sb("x_t", [P, KC, TM], F32)
    slots = [sb(f"wslot{j}", [P, 16384], BF16) for j in range(3)]
    A = overlay_alloc(nc, ctx, "arenaA", 16 * KB, [
        ("hb_t", [P, 16, TS], BF16, 0),
        ("gg_t", [P, KC, TM], BF16, 0),
    ])
    B = overlay_alloc(nc, ctx, "arenaB", 16 * KB, [
        ("xn_t", [P, KC, TM], BF16, 0),
        ("sf_t", [P, KC, TS], F32, 0),
    ])
    C = overlay_alloc(nc, ctx, "arenaC", 22 * KB, [
        ("xb_t", [P, KC, TS], BF16, 0),
        ("x2b_t", [P, 2, TS], BF16, 8 * KB),
        ("meanf_t", [P, TS], F32, 10 * KB),
        ("ex2_t", [P, TS], F32, 12 * KB),
        ("meanb_t", [P, TS], BF16, 14 * KB),
        ("rstdb_t", [P, TS], BF16, 15 * KB),
        ("tb_t", [P, TS], BF16, 16 * KB),
        ("erf_t", [P, TS], BF16, 17 * KB),
        ("al_t", [P, 2, TS], F32, 0),
        ("q_t", [P, 2, TS], F32, 4 * KB),
        ("u_t", [P, 2, TS], BF16, 8 * KB),
        ("hh_t", [P, 2, TS], F32, 10 * KB),
        ("si_t", [P, KC, TS], BF16, 14 * KB),
    ])
    Dv = overlay_alloc(nc, ctx, "arenaD", 8256, [
        ("xa_t", [P, KC, TS + 4], BF16, 0),
        ("y_t", [P, KC, TS], BF16, 0),
    ])
    xc_t = sb("xc_t", [P, KC, TM], BF16)
    ones_t = sb("ones_t", [P, P], BF16)
    bconv_t = sb("bconv_t", [P, KC], F32)
    fb_t = sb("fb_t", [P, KC], F32)
    c2_t = sb("c2_t", [P, KC], F32)
    cs1_t = sb("cs1_t", [P, KC], F32)
    cs2_t = sb("cs2_t", [P, KC], F32)
    bg_t = sb("bg_t", [P, 2 * KC], F32)
    b1_t = sb("b1_t", [P, DH // P], F32)
    b2_t = sb("b2_t", [P, KC], F32)
    b2h_t = sb("b2h_t", [P, KC], F32)
    wcf_t = sb("wcf_t", [P, KC * 4], F32)
    hc_t = sb("hc_t", [P, KC], F32)
    xatail_t = sb("xatail_t", [P, KC, 4], BF16)
    eps_t = sb("eps_t", [P, 1], F32)
    bg2_t = sb("bg2_t", [P, 2 * KC], F32)
    ch2_t = sb("ch2_t", [P, KC], F32)

    hb_t, gg_t = A["hb_t"], A["gg_t"]
    xn_t, sf_t = B["xn_t"], B["sf_t"]
    xb_t, x2b_t = C["xb_t"], C["x2b_t"]
    meanf_t, ex2_t, meanb_t, rstdb_t, tb_t, erf_t = (
        C["meanf_t"], C["ex2_t"], C["meanb_t"], C["rstdb_t"], C["tb_t"], C["erf_t"])
    al_t, q_t, u_t, hh_t, si_t = (
        C["al_t"], C["q_t"], C["u_t"], C["hh_t"], C["si_t"])
    xa_t, y_t = Dv["xa_t"], Dv["y_t"]

    ps = [ctx.enter_context(nc.psum_tensor(f"ps{i}", [P, TS], F32))
          for i in range(8)]

    prog = Prog(nc, ctx)

    LNKEYS = ["xb:%d" % c for c in range(KC)] + ["x2b0", "x2b1", "meanf",
              "ex2", "meanb", "rstdb", "tb", "erf"]
    H2KEYS = (["al0", "al1", "q0", "q1", "u0", "u1", "hh0", "hh1"]
              + ["sf:%d" % c for c in range(KC)] + ["si:%d" % c for c in range(KC)])
    HBKEYS = ["hb:%d" % m for m in range(16)]
    GGKEYS = ["gg:%d:%d" % (c, s) for c in range(KC) for s in range(NSUB)]
    YKEYS = ["y:%d" % c for c in range(KC)]
    XAKEYS = (["xa:%d" % c for c in range(KC)] + ["xat:%d" % c for c in range(KC)])
    XNKEYS = ["xn:%d:%d" % (c, s) for c in range(KC) for s in range(NSUB)]

    for name, tile, hb in (("bconv", bconv_t, bconv_h), ("fb", fb_t, fb_h),
                           ("bg", bg_t, bg_h), ("b1", b1_t, b1_h),
                           ("b2", b2_t, b2_h), ("wcf", wcf_t, wcf_h)):
        prog.dma("sync", "qs", lambda e, t=tile, h=hb: e.dma_start(out=t[:], in_=h[:]),
                 reads=(), writes=(name,))
    prog.comp("vector", lambda e: e.memset(ones_t[:], 1.0), writes=("ones",))
    prog.comp("vector", lambda e: e.memset(hc_t[:], 0.0),
              writes=tuple("hc:%d" % c for c in range(KC)))
    prog.comp("vector", lambda e: e.memset(xatail_t[:], 0.0),
              writes=tuple("xtl:%d" % c for c in range(KC)))
    prog.comp("vector", lambda e: e.tensor_scalar(b2h_t[:], b2_t[:], 0.5, None, ALU.mult),
              reads=("b2",), writes=("b2h",))
    prog.comp("vector", lambda e: e.tensor_scalar(cs1_t[:], fb_t[:], 0.0, None, ALU.max),
              reads=("fb",), writes=("cs1",))
    prog.comp("vector", lambda e: e.tensor_scalar(cs2_t[:], fb_t[:], 0.0, None, ALU.min),
              reads=("fb",), writes=("cs2",))
    prog.comp("vector", lambda e: e.tensor_tensor(cs2_t[:], cs2_t[:], cs1_t[:], ALU.subtract),
              reads=("cs1", "cs2"), writes=("cs2",))
    prog.comp("scalar", lambda e: e.activation(cs2_t[:], cs2_t[:], AF.Exp),
              reads=("cs2",), writes=("cs2",))
    prog.comp("vector", lambda e: e.tensor_scalar(cs2_t[:], cs2_t[:], 1.0, None, ALU.add),
              reads=("cs2",), writes=("cs2",))
    prog.comp("scalar", lambda e: e.activation(cs2_t[:], cs2_t[:], AF.Ln),
              reads=("cs2",), writes=("cs2",))
    prog.comp("vector", lambda e: e.tensor_tensor(cs1_t[:], cs1_t[:], cs2_t[:], ALU.add),
              reads=("cs1", "cs2"), writes=("cs1",))
    prog.comp("vector", lambda e: e.tensor_scalar(c2_t[:], cs1_t[:], -8.0, None, ALU.mult),
              reads=("cs1",), writes=("c2",))
    prog.comp("vector", lambda e: e.tensor_scalar(ch2_t[:], c2_t[:], 0.5, None, ALU.mult),
              reads=("c2",), writes=("ch2",))
    prog.comp("vector", lambda e: e.tensor_scalar(bg2_t[:], bg_t[:], 0.5, None, ALU.mult),
              reads=("bg",), writes=("bg2",))
    prog.comp("vector", lambda e: e.memset(eps_t[:], LN_EPS), writes=("eps",))

    CAST_LIST = [(w1_h, w1_s, D // P, DH), (w2_h, w2_s, DH // P, D),
                 (win_h, win_s, D // P, 2 * D), (wg_h, wg_s, D // P, 2 * D),
                 (wo_h, wo_s, D // P, D)]
    SCR_NAMES = ["scr_w1", "scr_w2", "scr_win", "scr_wg", "scr_wo"]

    def cast_weights(idx):
        scr_name = SCR_NAMES[idx]
        hsrc, sdst, nslab, cols = CAST_LIST[idx]
        for k0 in range(0, nslab, 4):
            k1 = min(k0 + 4, nslab)
            prog.dma("gpsimd", "qg",
                     lambda e, h=hsrc, sd=sdst, k0=k0, k1=k1: e.dma_start(
                         out=sd[k0 * P:k1 * P, :], in_=h[k0 * P:k1 * P, :]),
                     reads=(),
                     writes=tuple("%s:%d" % (scr_name, k) for k in range(k0, k1)))

    def grp_w1(h):
        src = w1_s.rearrange("(k p) c -> p k c", p=P)[:, :, h * 2048:(h + 1) * 2048]
        return src, 8, 2048, ["scr_w1:%d" % k for k in range(8)]

    def grp_w2(h):
        src = w2_s.rearrange("(k p) c -> p k c", p=P)[:, h * 16:(h + 1) * 16, :]
        return src, 16, 1024, ["scr_w2:%d" % k for k in range(h * 16, (h + 1) * 16)]

    def grp_win():
        return (win_s.rearrange("(k p) c -> p k c", p=P), 8, 2048,
                ["scr_win:%d" % k for k in range(8)])

    def grp_wg():
        return (wg_s.rearrange("(k p) c -> p k c", p=P), 8, 2048,
                ["scr_wg:%d" % k for k in range(8)])

    def grp_wo():
        return (wo_s.rearrange("(k p) c -> p k c", p=P), 8, 1024,
                ["scr_wo:%d" % k for k in range(8)])

    wf1 = w1_h.rearrange("(k p) c -> p k c", p=P)
    wf2 = w2_h.rearrange("(k p) c -> p k c", p=P)
    DIRECT_F32 = {
        0: (wf1[:, :, 0:2048], 8, 2048),
        1: (wf2[:, 0:16, :], 16, 1024),
        2: (wf1[:, :, 2048:4096], 8, 2048),
        3: (wf2[:, 16:32, :], 16, 1024),
        4: (win_h.rearrange("(k p) c -> p k c", p=P), 8, 2048),
        5: (wg_h.rearrange("(k p) c -> p k c", p=P), 8, 2048),
        6: (wo_h.rearrange("(k p) c -> p k c", p=P), 8, 1024),
    }

    def load_group(gidx, grp):
        j = gidx % 3
        if gidx in DIRECT_F32:
            src, nk, cols = DIRECT_F32.pop(gidx)
            dst = slots[j][:, 0:nk * cols].rearrange("p (k c) -> p k c", c=cols)
            prog.dma("gpsimd", "qg", lambda e, d=dst, s=src: e.dma_start(out=d, in_=s),
                     reads=(), writes=("slot%d" % j,))
            return
        src, nk, cols, rds = grp
        dst = slots[j][:, 0:nk * cols].rearrange("p (k c) -> p k c", c=cols)
        prog.dma("sync", "qs", lambda e, d=dst, s=src: e.dma_start(out=d, in_=s),
                 reads=tuple(rds), writes=("slot%d" % j,))

    def lhs(gidx, nk_cols, k, m):
        j = gidx % 3
        cols = nk_cols
        return slots[j][:, k * cols + m * P: k * cols + (m + 1) * P]

    def x_load(m, s):
        seq, t0 = m // 2, (m % 2) * TM
        src = x_h[seq].rearrange("(c p) t -> p c t", p=P)[:, :, t0 + s * TS: t0 + (s + 1) * TS]
        prog.dma("sync", "qs",
                 lambda e, s_=src, d=x_t[:, :, s * TS:(s + 1) * TS]: e.dma_start(out=d, in_=s_),
                 reads=(), writes=("x:%d" % s,))

    def x_store(m, s):
        seq, t0 = m // 2, (m % 2) * TM
        dst = out_h[seq].rearrange("(c p) t -> p c t", p=P)[:, :, t0 + s * TS: t0 + (s + 1) * TS]
        prog.dma("sync", "qs",
                 lambda e, d=dst, s_=x_t[:, :, s * TS:(s + 1) * TS]: e.dma_start(out=d, in_=s_),
                 reads=("x:%d" % s,), writes=("xout:%d:%d" % (m, s),))

    def layer_norm(s):
        off = s * TS
        for c in range(KC):
            xs = x_t[:, c, off:off + TS]
            prog.comp("vector", lambda e, a=xb_t[:, c, :], b=xs: e.tensor_copy(a, b),
                      reads=("x:%d" % s,), writes=("xb:%d" % c,))
            prog.comp("vector",
                      lambda e, a=x2b_t[:, c % 2, :], b=xb_t[:, c, :]: e.tensor_tensor(a, b, b, ALU.mult),
                      reads=("xb:%d" % c,), writes=("x2b%d" % (c % 2),))
            prog.comp("tensor",
                      lambda e, b=xb_t[:, c, :], st=(c == 0), sp=(c == KC - 1):
                      e.matmul(ps[6][:], ones_t[:], b, start=st, stop=sp,
                               skip_group_check=True),
                      reads=("ones", "xb:%d" % c), writes=("ps6",))
            prog.comp("tensor",
                      lambda e, b=x2b_t[:, c % 2, :], st=(c == 0), sp=(c == KC - 1):
                      e.matmul(ps[7][:], ones_t[:], b, start=st, stop=sp,
                               skip_group_check=True),
                      reads=("ones", "x2b%d" % (c % 2)), writes=("ps7",))
        inv = 1.0 / D
        prog.comp("scalar", lambda e: e.activation(ex2_t[:], ps[7][:], AF.Relu,
                                                   scale=inv, bias=eps_t[:, 0:1]),
                  reads=("ps7", "eps"), writes=("ex2",))
        prog.comp("scalar", lambda e: e.activation(meanf_t[:], ps[6][:], AF.Square, scale=inv),
                  reads=("ps6",), writes=("meanf",))
        prog.comp("scalar", lambda e: e.activation(meanb_t[:], ps[6][:], AF.Copy, scale=inv),
                  reads=("ps6",), writes=("meanb",))
        prog.comp("vector", lambda e: e.tensor_tensor(ex2_t[:], ex2_t[:], meanf_t[:], ALU.subtract),
                  reads=("ex2", "meanf"), writes=("ex2",))
        prog.comp("scalar", lambda e: e.activation(ex2_t[:], ex2_t[:], AF.Ln),
                  reads=("ex2",), writes=("ex2",))
        prog.comp("scalar", lambda e: e.activation(rstdb_t[:], ex2_t[:], AF.Exp, scale=-0.5),
                  reads=("ex2",), writes=("rstdb",))
        for c in range(KC):
            prog.comp("vector", lambda e, c=c: e.tensor_tensor(tb_t[:], xb_t[:, c, :], meanb_t[:], ALU.subtract),
                      reads=("xb:%d" % c, "meanb"), writes=("tb",))
            prog.comp("vector",
                      lambda e, c=c, o=off: e.tensor_tensor(xn_t[:, c, o:o + TS], tb_t[:], rstdb_t[:], ALU.mult),
                      reads=("tb", "rstdb"), writes=("xn:%d:%d" % (c, s),))

    def ffn_half(m, which, h, g1, g2, hook=None):
        for s in range(NSUB):
            off = s * TS
            for mm in range(16):
                bank = mm % 4
                for k in range(KC):
                    prog.comp("tensor",
                              lambda e, l=lhs(g1, 2048, k, mm), r=xn_t[:, k, off:off + TS],
                              st=(k == 0), sp=(k == KC - 1), b=bank:
                              e.matmul(ps[b][:], l, r, start=st, stop=sp),
                              reads=("slot%d" % (g1 % 3), "xn:%d:%d" % (k, s)),
                              writes=("ps%d" % bank,))
                bidx = h * 16 + mm
                prog.comp("scalar",
                          lambda e, b=bank, mm=mm, bi=bidx:
                          e.activation(hb_t[:, mm, :], ps[b][:], AF.Relu, bias=b1_t[:, bi:bi + 1]),
                          reads=("ps%d" % bank, "b1"), writes=("hb:%d" % mm,))
            for mo in range(KC):
                bank = 4 + mo % 2
                for k in range(16):
                    prog.comp("tensor",
                              lambda e, l=lhs(g2, 1024, k, mo), r=hb_t[:, k, :],
                              st=(k == 0), sp=(k == 15), b=bank:
                              e.matmul(ps[b][:], l, r, start=st, stop=sp),
                              reads=("slot%d" % (g2 % 3), "hb:%d" % k),
                              writes=("ps%d" % bank,))
                xs = x_t[:, mo, off:off + TS]
                prog.comp("vector",
                          lambda e, b=bank, xs=xs:
                          e.scalar_tensor_tensor(xs, ps[b][:], 0.5, xs, ALU.mult, ALU.add),
                          reads=("ps%d" % bank, "x:%d" % s), writes=("x:%d" % s,))
                if h == 0:
                    prog.comp("vector",
                              lambda e, xs=xs, mo=mo:
                              e.tensor_scalar(xs, xs, b2h_t[:, mo:mo + 1], None, ALU.add),
                              reads=("x:%d" % s, "b2h"), writes=("x:%d" % s,))
            if hook is not None:
                hook(s)

    def hawk_h1(m, g_in, skip_ln=False):
        prog.fence(HBKEYS)
        seq_start = (m % 2 == 0)
        if seq_start:
            prog.comp("vector", lambda e: e.memset(xatail_t[:], 0.0),
                      writes=tuple("xtl:%d" % c for c in range(KC)))
            prog.comp("vector", lambda e: e.memset(hc_t[:], 0.0),
                      writes=tuple("hc:%d" % c for c in range(KC)))
        if not skip_ln:
            for s in range(NSUB):
                layer_norm(s)
        for s in range(NSUB):
            off = s * TS
            for c in range(KC):
                prog.comp("vector",
                          lambda e, c=c: e.tensor_copy(xa_t[:, c, 0:4], xatail_t[:, c, :]),
                          reads=("xtl:%d" % c,), writes=("xat:%d" % c,))
            for mm in range(16):
                bank = mm % 4
                for k in range(KC):
                    prog.comp("tensor",
                              lambda e, l=lhs(g_in, 2048, k, mm), r=xn_t[:, k, off:off + TS],
                              st=(k == 0), sp=(k == KC - 1), b=bank:
                              e.matmul(ps[b][:], l, r, start=st, stop=sp),
                              reads=("slot%d" % (g_in % 3), "xn:%d:%d" % (k, s)),
                              writes=("ps%d" % bank,))
                if mm < KC:
                    c = mm
                    prog.comp("scalar",
                              lambda e, b=bank, c=c, o=off:
                              e.activation(gg_t[:, c, o:o + TS], ps[b][:], AF.Gelu),
                              reads=("ps%d" % bank,), writes=("gg:%d:%d" % (c, s),))
                else:
                    c = mm - KC
                    prog.comp("scalar",
                              lambda e, b=bank, c=c: e.activation(xa_t[:, c, 4:4 + TS], ps[b][:], AF.Copy),
                              reads=("ps%d" % bank,), writes=("xa:%d" % c,))
                    ceng = "vector"
                    xcs = xc_t[:, c, off:off + TS]
                    prog.comp(ceng,
                              lambda e, c=c, xcs=xcs: e.tensor_scalar(
                                  xcs, xa_t[:, c, 1:1 + TS], wcf_t[:, c * 4:c * 4 + 1],
                                  bconv_t[:, c:c + 1], ALU.mult, ALU.add),
                              reads=("xa:%d" % c, "xat:%d" % c, "wcf", "bconv"),
                              writes=("xc:%d:%d" % (c, s),))
                    for kk in range(1, 4):
                        prog.comp(ceng,
                                  lambda e, c=c, kk=kk, xcs=xcs: e.scalar_tensor_tensor(
                                      xcs, xa_t[:, c, 1 + kk:1 + kk + TS],
                                      wcf_t[:, c * 4 + kk:c * 4 + kk + 1], xcs, ALU.mult, ALU.add),
                                  reads=("xa:%d" % c, "xat:%d" % c, "wcf", "xc:%d:%d" % (c, s)),
                                  writes=("xc:%d:%d" % (c, s),))
                    prog.comp(ceng,
                              lambda e, c=c: e.tensor_copy(xatail_t[:, c, :], xa_t[:, c, TS:TS + 4]),
                              reads=("xa:%d" % c,), writes=("xtl:%d" % c,))

    def h2_g_block(s, g_g):
        off = s * TS
        for c in range(KC):
            bankF, bankI = c % 2, 2 + c % 2
            for k in range(KC):
                prog.comp("tensor",
                          lambda e, l=lhs(g_g, 2048, k, c), r=xc_t[:, k, off:off + TS],
                          st=(k == 0), sp=(k == KC - 1), b=bankF:
                          e.matmul(ps[b][:], l, r, start=st, stop=sp),
                          reads=("slot%d" % (g_g % 3), "xc:%d:%d" % (k, s)),
                          writes=("ps%d" % bankF,))
            for k in range(KC):
                prog.comp("tensor",
                          lambda e, l=lhs(g_g, 2048, k, KC + c), r=xc_t[:, k, off:off + TS],
                          st=(k == 0), sp=(k == KC - 1), b=bankI:
                          e.matmul(ps[b][:], l, r, start=st, stop=sp),
                          reads=("slot%d" % (g_g % 3), "xc:%d:%d" % (k, s)),
                          writes=("ps%d" % bankI,))
            prog.comp("scalar",
                      lambda e, b=bankF, c=c: e.activation(
                          sf_t[:, c, :], ps[b][:], AF.Tanh, scale=0.5,
                          bias=bg2_t[:, c:c + 1]),
                      reads=("ps%d" % bankF, "bg2"), writes=("sf:%d" % c,))
            prog.comp("scalar",
                      lambda e, b=bankI, c=c: e.activation(
                          si_t[:, c, :], ps[b][:], AF.Tanh, scale=0.5,
                          bias=bg2_t[:, KC + c:KC + c + 1]),
                      reads=("ps%d" % bankI, "bg2"), writes=("si:%d" % c,))
            prog.comp("scalar",
                      lambda e, c=c: e.activation(
                          sf_t[:, c, :], sf_t[:, c, :], AF.Exp,
                          scale=ch2_t[:, c:c + 1], bias=ch2_t[:, c:c + 1]),
                      reads=("sf:%d" % c, "ch2"), writes=("sf:%d" % c,))

    def h2_chains(s):
        off = s * TS

        def tail(c):
            pp = c % 2
            prog.comp("vector",
                      lambda e, c=c, pp=pp: e.tensor_tensor_scan(
                          hh_t[:, pp, :], sf_t[:, c, :], u_t[:, pp, :], hc_t[:, c:c + 1], ALU.mult, ALU.add),
                      reads=("sf:%d" % c, "u%d" % pp, "hc:%d" % c), writes=("hh%d" % pp,))
            prog.comp("gpsimd",
                      lambda e, c=c, pp=pp: e.tensor_copy(hc_t[:, c:c + 1], hh_t[:, pp, TS - 1:TS]),
                      reads=("hh%d" % pp,), writes=("hc:%d" % c,))
            prog.comp("gpsimd",
                      lambda e, c=c, pp=pp, o=off: e.tensor_tensor(
                          y_t[:, c, :], gg_t[:, c, o:o + TS], hh_t[:, pp, :], ALU.mult),
                      reads=("gg:%d:%d" % (c, s), "hh%d" % pp), writes=("y:%d" % c,))

        for c in range(KC):
            pp = c % 2
            prog.comp("scalar",
                      lambda e, c=c, pp=pp: e.activation(
                          q_t[:, pp, :], sf_t[:, c, :], AF.Square),
                      reads=("sf:%d" % c,), writes=("q%d" % pp,))
            prog.comp("vector",
                      lambda e, pp=pp: e.tensor_scalar(q_t[:, pp, :], q_t[:, pp, :], -0.25, 0.25000025, ALU.mult, ALU.add),
                      reads=("q%d" % pp,), writes=("q%d" % pp,))
            prog.comp("scalar",
                      lambda e, pp=pp: e.activation(q_t[:, pp, :], q_t[:, pp, :], AF.Sqrt),
                      reads=("q%d" % pp,), writes=("q%d" % pp,))
            prog.comp("vector",
                      lambda e, c=c, pp=pp, o=off: e.scalar_tensor_tensor(
                          u_t[:, pp, :], si_t[:, c, :], 1.0, xc_t[:, c, o:o + TS], ALU.add, ALU.mult),
                      reads=("si:%d" % c, "xc:%d:%d" % (c, s)), writes=("u%d" % pp,))
            prog.comp("gpsimd",
                      lambda e, pp=pp: e.tensor_tensor(u_t[:, pp, :], u_t[:, pp, :], q_t[:, pp, :], ALU.mult),
                      reads=("u%d" % pp, "q%d" % pp), writes=("u%d" % pp,))
            if c >= 1:
                tail(c - 1)
        tail(KC - 1)

    def h2_wout(s, g_o):
        off = s * TS
        for mo in range(KC):
            bank = 4 + mo % 2
            for k in range(KC):
                prog.comp("tensor",
                          lambda e, l=lhs(g_o, 1024, k, mo), r=y_t[:, k, :],
                          st=(k == 0), sp=(k == KC - 1), b=bank:
                          e.matmul(ps[b][:], l, r, start=st, stop=sp),
                          reads=("slot%d" % (g_o % 3), "y:%d" % k),
                          writes=("ps%d" % bank,))
            xs = x_t[:, mo, off:off + TS]
            prog.comp("vector",
                      lambda e, b=bank, xs=xs: e.tensor_tensor(xs, xs, ps[b][:], ALU.add),
                      reads=("ps%d" % bank, "x:%d" % s), writes=("x:%d" % s,))

    def hawk_h2(m, g_g, g_o):
        prog.fence(XNKEYS + LNKEYS + XAKEYS)
        h2_g_block(0, g_g)
        h2_chains(0)
        h2_g_block(1, g_g)
        h2_wout(0, g_o)
        h2_chains(1)
        h2_wout(1, g_o)

    gidx = 0

    def next_gidx():
        nonlocal gidx
        g = gidx
        gidx += 1
        return g

    for s in range(NSUB):
        x_load(0, s)
    g_next = [next_gidx(), next_gidx()]
    load_group(g_next[0], grp_w1(0))
    load_group(g_next[1], grp_w2(0))

    nmacro = DEBUG_NMACRO or NMACRO
    full = DEBUG_PHASES == "all"
    ln1_done = False
    for m in range(nmacro):
        g_w1a, g_w2a = g_next
        g_w1b, g_w2b = next_gidx(), next_gidx()
        load_group(g_w1b, grp_w1(1))
        if m > 0:
            prog.fence(GGKEYS + H2KEYS + YKEYS)
        if not ln1_done:
            layer_norm(0)
            layer_norm(1)
            f1a_hook = None
        else:
            f1a_hook = lambda s: layer_norm(1) if s == 0 else None
        ffn_half(m, 1, 0, g_w1a, g_w2a, hook=f1a_hook)
        load_group(g_w2b, grp_w2(1))
        g_in = next_gidx()
        load_group(g_in, grp_win())
        ffn_half(m, 1, 1, g_w1b, g_w2b,
                 hook=(layer_norm if full else None))
        g_g, g_o = next_gidx(), next_gidx()
        load_group(g_g, grp_wg())
        load_group(g_o, grp_wo())
        if m == 0:
            for ci in range(5):
                cast_weights(ci)
        if DEBUG_PHASES in ("f1+hawk", "all"):
            hawk_h1(m, g_in, skip_ln=full)
        g_w1a2, g_w2a2 = next_gidx(), next_gidx()
        load_group(g_w1a2, grp_w1(0))
        if DEBUG_PHASES in ("f1+hawk", "all"):
            hawk_h2(m, g_g, g_o)
        load_group(g_w2a2, grp_w2(0))
        g_w1b2, g_w2b2 = next_gidx(), next_gidx()
        load_group(g_w1b2, grp_w1(1))
        if full:
            prog.fence(GGKEYS + H2KEYS + YKEYS)
            layer_norm(0)
            layer_norm(1)
            ffn_half(m, 2, 0, g_w1a2, g_w2a2)
        load_group(g_w2b2, grp_w2(1))
        if m < nmacro - 1:
            g_next = [next_gidx(), next_gidx()]
            load_group(g_next[0], grp_w1(0))

        def f2b_hook(s, m=m):
            x_store(m, s)
            if m < nmacro - 1:
                x_load(m + 1, s)

        if full:
            ffn_half(m, 2, 1, g_w1b2, g_w2b2, hook=f2b_hook)
            if m < nmacro - 1:
                layer_norm(0)
            ln1_done = m < nmacro - 1
        else:
            prog.fence(HBKEYS)
            for s in range(NSUB):
                x_store(m, s)
                if m < nmacro - 1:
                    x_load(m + 1, s)
        if m < nmacro - 1:
            load_group(g_next[1], grp_w2(0))

    prog.final_wait("sync", ["xout:%d:%d" % (m, s)
                             for m in range(NMACRO) for s in range(NSUB)])

    with nc.Block() as block:
        prog.emit(block)
    return nc


_NC_CACHE = None


def _get_nc():
    global _NC_CACHE
    if _NC_CACHE is None:
        _NC_CACHE = build_nc()
    return _NC_CACHE


def kernel(L0_attn_x, L0_attn_w_in, L0_attn_w_conv, L0_attn_b_conv,
           L0_attn_w_gates, L0_attn_b_gates, L0_attn_forget_base,
           L0_attn_w_out, L0_ffn_w1, L0_ffn_b1, L0_ffn_w2, L0_ffn_b2):
    x = np.asarray(L0_attn_x, dtype=np.float32)
    assert x.shape == (32, T, D)

    def vecP(v, n):
        return np.ascontiguousarray(np.asarray(v, np.float32).reshape(n, P).T)

    common = {
        "w1": np.ascontiguousarray(np.asarray(L0_ffn_w1, np.float32)),
        "w2": np.ascontiguousarray(np.asarray(L0_ffn_w2, np.float32)),
        "winT": np.ascontiguousarray(np.asarray(L0_attn_w_in, np.float32).T),
        "wgT": np.ascontiguousarray(np.asarray(L0_attn_w_gates, np.float32).T),
        "woT": np.ascontiguousarray(np.asarray(L0_attn_w_out, np.float32).T),
        "bconv": vecP(L0_attn_b_conv, KC),
        "fb": vecP(L0_attn_forget_base, KC),
        "bg": vecP(L0_attn_b_gates, 2 * KC),
        "b1": vecP(L0_ffn_b1, DH // P),
        "b2": vecP(L0_ffn_b2, KC),
        "wcf": np.ascontiguousarray(
            np.asarray(L0_attn_w_conv, np.float32)[:, 0, :]
            .reshape(KC, P, 4).transpose(1, 0, 2).reshape(P, KC * 4)),
    }
    in_maps = []
    for core in range(8):
        shard = x[core * SEQ_PER_CORE:(core + 1) * SEQ_PER_CORE]
        xs = np.ascontiguousarray(shard.transpose(0, 2, 1))
        im = dict(common)
        im["x"] = xs
        in_maps.append(im)

    nc = _get_nc()
    res = run_bass_kernel_spmd(nc, in_maps, core_ids=list(range(8)))
    outs = []
    for core in range(8):
        o = np.asarray(res.results[core]["xout"])
        outs.append(o.transpose(0, 2, 1))
    return np.ascontiguousarray(np.concatenate(outs, axis=0)).astype(np.float32)


# version 23
# speedup vs baseline: 1.0304x; 1.0304x over previous
import numpy as np
from contextlib import ExitStack

import concourse.bass as bass
import concourse.mybir as mybir
from concourse.bass_utils import run_bass_kernel_spmd

F32 = mybir.dt.float32
BF16 = mybir.dt.bfloat16
AF = mybir.ActivationFunctionType
ALU = mybir.AluOpType

P = 128
D = 1024
KC = D // P
DH = 4096
T = 2048
SEQ_PER_CORE = 4
TM = 1024
TS = 512
NSUB = TM // TS
NMACRO = SEQ_PER_CORE * T // TM
INV_SQRT2 = 0.7071067811865476
LN_EPS = 1e-5

ENGINES = ("tensor", "vector", "scalar", "gpsimd", "sync")

DEBUG_PHASES = "all"
DEBUG_NMACRO = None


class Prog:

    def __init__(self, nc, ctx):
        self.nc = nc
        self.ops = []
        self.eng_ops = {e: [] for e in ENGINES}
        self.last_write = {}
        self.readers = {}
        self.esem = {e: ctx.enter_context(nc.semaphore(f"se_{e}"))
                     for e in ("tensor", "vector", "scalar", "gpsimd")}
        self.NLANE = 8
        self.qsem = {q: [ctx.enter_context(nc.semaphore(f"sq_{q}{i}"))
                         for i in range(self.NLANE)]
                     for q in ("qs", "qg")}
        self.qcount = {q: 0 for q in ("qs", "qg")}
        self.pending_fence = {e: set() for e in ENGINES}

    def _stream(self, op):
        if op["kind"] == "dma":
            return (op["queue"], op["lane"])
        return op["eng"]

    def _record(self, eng, fn, reads, writes, kind, queue=None):
        opid = len(self.ops)
        op = dict(id=opid, eng=eng, fn=fn, kind=kind, queue=queue,
                  need_tok=False, eidx=len(self.eng_ops[eng]))
        if kind == "dma":
            op["qidx"] = self.qcount[queue]
            op["lane"] = op["qidx"] % self.NLANE
            op["cycle"] = op["qidx"] // self.NLANE
            self.qcount[queue] += 1
        sid = self._stream(op)
        deps = set(self.pending_fence[eng])
        self.pending_fence[eng] = set()
        for r in reads:
            w = self.last_write.get(r)
            if w is not None:
                deps.add(w)
            self.readers.setdefault(r, {})[sid] = opid
        for w in writes:
            lw = self.last_write.get(w)
            if lw is not None:
                deps.add(lw)
            for rd in self.readers.get(w, {}).values():
                if rd != opid:
                    deps.add(rd)
            self.last_write[w] = opid
            self.readers[w] = {}
        op["deps"] = deps
        self.ops.append(op)
        self.eng_ops[eng].append(op)
        return opid

    def comp(self, eng, fn, reads=(), writes=()):
        return self._record(eng, fn, reads, writes, "comp")

    def dma(self, eng, queue, fn, reads=(), writes=()):
        return self._record(eng, fn, reads, writes, "dma", queue=queue)

    def fence(self, keys):
        dep = set()
        for k in keys:
            w = self.last_write.get(k)
            if w is not None:
                dep.add(w)
            dep.update(self.readers.get(k, {}).values())
        for e in ENGINES:
            self.pending_fence[e] |= dep

    def final_wait(self, eng, keys):
        self._record(eng, None, keys, (), "final")

    def emit(self, block):
        for op in self.ops:
            for d in op["deps"]:
                prod = self.ops[d]
                if prod["kind"] == "dma":
                    pass
                elif prod["eng"] == op["eng"] and op["eng"] == "tensor":
                    continue
                else:
                    prod["need_tok"] = True
        for e in ("tensor", "vector", "scalar", "gpsimd"):
            c = 0
            for op in self.eng_ops[e]:
                if op["kind"] == "comp" and op["need_tok"]:
                    c += 1
                op["tok"] = c
        hw = {e: {} for e in ENGINES}
        for op in self.ops:
            waits = {}
            if op["kind"] == "dma" and op["cycle"] >= 1:
                sem = self.qsem[op["queue"]][op["lane"]]
                waits[id(sem)] = (sem, 16 * op["cycle"])
            for d in sorted(op["deps"]):
                prod = self.ops[d]
                if prod["kind"] == "dma":
                    sem = self.qsem[prod["queue"]][prod["lane"]]
                    val = 16 * (prod["cycle"] + 1)
                elif prod["eng"] == op["eng"] and op["eng"] == "tensor":
                    continue
                else:
                    sem, val = self.esem[prod["eng"]], prod["tok"]
                    if val == 0:
                        continue
                k = id(sem)
                waits[k] = (sem, max(val, waits[k][1]) if k in waits else val)
            op["waits"] = []
            for k, (sem, val) in waits.items():
                if hw[op["eng"]].get(k, -1) >= val:
                    continue
                hw[op["eng"]][k] = val
                op["waits"].append((sem, val))

        def body(engine_name):
            def run(eng):
                for op in self.eng_ops[engine_name]:
                    for sem, val in op["waits"]:
                        eng.wait_ge(sem, val)
                    if op["fn"] is None:
                        continue
                    instr = op["fn"](eng)
                    if op["kind"] == "dma":
                        instr.then_inc(self.qsem[op["queue"]][op["lane"]], 16)
                    elif op["need_tok"]:
                        instr.then_inc(self.esem[op["eng"]], 1)
            return run

        block.tensor(body("tensor"))
        block.vector(body("vector"))
        block.scalar(body("scalar"))
        block.gpsimd(body("gpsimd"))
        block.sync(body("sync"))


def overlay_alloc(nc, ctx, arena_name, size_bytes, members):
    arena = ctx.enter_context(
        nc.sbuf_tensor(arena_name, [P, size_bytes], mybir.dt.uint8))
    out = {}
    for (name, shape, dtype, off) in members:
        esz = mybir.dt.size(dtype)
        nbytes = int(np.prod(shape[1:])) * esz
        assert off % 4 == 0 and off + nbytes <= size_bytes, (name, off, nbytes)
        ap = arena[:, off:off + nbytes].bitcast(dtype)
        if len(shape) == 3:
            ap = ap.rearrange("p (a b) -> p a b", b=shape[2])
        else:
            assert len(shape) == 2 and ap.shape[1] == shape[1]
        out[name] = ap
    return out


def build_nc(race_detect=True):
    nc = bass.Bass("TRN2", target_bir_lowering=False,
                   detect_race_conditions=race_detect)

    x_h = nc.dram_tensor("x", [SEQ_PER_CORE, D, T], F32, kind="ExternalInput").ap()
    w1_h = nc.dram_tensor("w1", [D, DH], F32, kind="ExternalInput").ap()
    w2_h = nc.dram_tensor("w2", [DH, D], F32, kind="ExternalInput").ap()
    win_h = nc.dram_tensor("winT", [D, 2 * D], F32, kind="ExternalInput").ap()
    wg_h = nc.dram_tensor("wgT", [D, 2 * D], F32, kind="ExternalInput").ap()
    wo_h = nc.dram_tensor("woT", [D, D], F32, kind="ExternalInput").ap()
    bconv_h = nc.dram_tensor("bconv", [P, KC], F32, kind="ExternalInput").ap()
    fb_h = nc.dram_tensor("fb", [P, KC], F32, kind="ExternalInput").ap()
    bg_h = nc.dram_tensor("bg", [P, 2 * KC], F32, kind="ExternalInput").ap()
    b1_h = nc.dram_tensor("b1", [P, DH // P], F32, kind="ExternalInput").ap()
    b2_h = nc.dram_tensor("b2", [P, KC], F32, kind="ExternalInput").ap()
    wcf_h = nc.dram_tensor("wcf", [P, KC * 4], F32, kind="ExternalInput").ap()
    out_h = nc.dram_tensor("xout", [SEQ_PER_CORE, D, T], F32, kind="ExternalOutput").ap()

    w1_s = nc.dram_tensor("w1_s", [D, DH], BF16).ap()
    w2_s = nc.dram_tensor("w2_s", [DH, D], BF16).ap()
    win_s = nc.dram_tensor("win_s", [D, 2 * D], BF16).ap()
    wg_s = nc.dram_tensor("wg_s", [D, 2 * D], BF16).ap()
    wo_s = nc.dram_tensor("wo_s", [D, D], BF16).ap()

    ctx = ExitStack()
    nc._kernel_ctx = ctx

    sb = lambda name, shape, dt: ctx.enter_context(nc.sbuf_tensor(name, shape, dt))
    KB = 1024
    x_t = sb("x_t", [P, KC, TM], F32)
    slots = [sb(f"wslot{j}", [P, 16384], BF16) for j in range(3)]
    A = overlay_alloc(nc, ctx, "arenaA", 16 * KB, [
        ("hb_t", [P, 16, TS], BF16, 0),
        ("gg_t", [P, KC, TM], BF16, 0),
    ])
    B = overlay_alloc(nc, ctx, "arenaB", 16 * KB, [
        ("xn_t", [P, KC, TM], BF16, 0),
        ("sf_t", [P, KC, TS], F32, 0),
    ])
    C = overlay_alloc(nc, ctx, "arenaC", 22 * KB, [
        ("xb_t", [P, KC, TS], BF16, 0),
        ("x2b_t", [P, 2, TS], BF16, 8 * KB),
        ("meanf_t", [P, TS], F32, 10 * KB),
        ("ex2_t", [P, TS], F32, 12 * KB),
        ("meanb_t", [P, TS], BF16, 14 * KB),
        ("rstdb_t", [P, TS], BF16, 15 * KB),
        ("tb_t", [P, TS], BF16, 16 * KB),
        ("erf_t", [P, TS], BF16, 17 * KB),
        ("al_t", [P, 2, TS], F32, 0),
        ("q_t", [P, 2, TS], F32, 4 * KB),
        ("u_t", [P, 2, TS], BF16, 8 * KB),
        ("hh_t", [P, 2, TS], F32, 10 * KB),
        ("si_t", [P, KC, TS], BF16, 14 * KB),
    ])
    Dv = overlay_alloc(nc, ctx, "arenaD", 8256, [
        ("xa_t", [P, KC, TS + 4], BF16, 0),
        ("y_t", [P, KC, TS], BF16, 0),
    ])
    xc_t = sb("xc_t", [P, KC, TM], BF16)
    ones_t = sb("ones_t", [P, P], BF16)
    bconv_t = sb("bconv_t", [P, KC], F32)
    fb_t = sb("fb_t", [P, KC], F32)
    c2_t = sb("c2_t", [P, KC], F32)
    cs1_t = sb("cs1_t", [P, KC], F32)
    cs2_t = sb("cs2_t", [P, KC], F32)
    bg_t = sb("bg_t", [P, 2 * KC], F32)
    b1_t = sb("b1_t", [P, DH // P], F32)
    b2_t = sb("b2_t", [P, KC], F32)
    b2h_t = sb("b2h_t", [P, KC], F32)
    wcf_t = sb("wcf_t", [P, KC * 4], F32)
    hc_t = sb("hc_t", [P, KC], F32)
    xatail_t = sb("xatail_t", [P, KC, 4], BF16)
    eps_t = sb("eps_t", [P, 1], F32)
    bg2_t = sb("bg2_t", [P, 2 * KC], F32)
    ch2_t = sb("ch2_t", [P, KC], F32)

    hb_t, gg_t = A["hb_t"], A["gg_t"]
    xn_t, sf_t = B["xn_t"], B["sf_t"]
    xb_t, x2b_t = C["xb_t"], C["x2b_t"]
    meanf_t, ex2_t, meanb_t, rstdb_t, tb_t, erf_t = (
        C["meanf_t"], C["ex2_t"], C["meanb_t"], C["rstdb_t"], C["tb_t"], C["erf_t"])
    al_t, q_t, u_t, hh_t, si_t = (
        C["al_t"], C["q_t"], C["u_t"], C["hh_t"], C["si_t"])
    xa_t, y_t = Dv["xa_t"], Dv["y_t"]

    ps = [ctx.enter_context(nc.psum_tensor(f"ps{i}", [P, TS], F32))
          for i in range(8)]

    prog = Prog(nc, ctx)

    LNKEYS = ["xb:%d" % c for c in range(KC)] + ["x2b0", "x2b1", "meanf",
              "ex2", "meanb", "rstdb", "tb", "erf"]
    H2KEYS = (["al0", "al1", "q0", "q1", "u0", "u1", "hh0", "hh1"]
              + ["sf:%d" % c for c in range(KC)] + ["si:%d" % c for c in range(KC)])
    HBKEYS = ["hb:%d" % m for m in range(16)]
    GGKEYS = ["gg:%d:%d" % (c, s) for c in range(KC) for s in range(NSUB)]
    YKEYS = ["y:%d" % c for c in range(KC)]
    XAKEYS = (["xa:%d" % c for c in range(KC)] + ["xat:%d" % c for c in range(KC)])
    XNKEYS = ["xn:%d:%d" % (c, s) for c in range(KC) for s in range(NSUB)]

    for name, tile, hb in (("bconv", bconv_t, bconv_h), ("fb", fb_t, fb_h),
                           ("bg", bg_t, bg_h), ("b1", b1_t, b1_h),
                           ("b2", b2_t, b2_h), ("wcf", wcf_t, wcf_h)):
        prog.dma("sync", "qs", lambda e, t=tile, h=hb: e.dma_start(out=t[:], in_=h[:]),
                 reads=(), writes=(name,))
    prog.comp("vector", lambda e: e.memset(ones_t[:], 1.0), writes=("ones",))
    prog.comp("vector", lambda e: e.memset(hc_t[:], 0.0),
              writes=tuple("hc:%d" % c for c in range(KC)))
    prog.comp("vector", lambda e: e.memset(xatail_t[:], 0.0),
              writes=tuple("xtl:%d" % c for c in range(KC)))
    prog.comp("vector", lambda e: e.tensor_scalar(b2h_t[:], b2_t[:], 0.5, None, ALU.mult),
              reads=("b2",), writes=("b2h",))
    prog.comp("vector", lambda e: e.tensor_scalar(cs1_t[:], fb_t[:], 0.0, None, ALU.max),
              reads=("fb",), writes=("cs1",))
    prog.comp("vector", lambda e: e.tensor_scalar(cs2_t[:], fb_t[:], 0.0, None, ALU.min),
              reads=("fb",), writes=("cs2",))
    prog.comp("vector", lambda e: e.tensor_tensor(cs2_t[:], cs2_t[:], cs1_t[:], ALU.subtract),
              reads=("cs1", "cs2"), writes=("cs2",))
    prog.comp("scalar", lambda e: e.activation(cs2_t[:], cs2_t[:], AF.Exp),
              reads=("cs2",), writes=("cs2",))
    prog.comp("vector", lambda e: e.tensor_scalar(cs2_t[:], cs2_t[:], 1.0, None, ALU.add),
              reads=("cs2",), writes=("cs2",))
    prog.comp("scalar", lambda e: e.activation(cs2_t[:], cs2_t[:], AF.Ln),
              reads=("cs2",), writes=("cs2",))
    prog.comp("vector", lambda e: e.tensor_tensor(cs1_t[:], cs1_t[:], cs2_t[:], ALU.add),
              reads=("cs1", "cs2"), writes=("cs1",))
    prog.comp("vector", lambda e: e.tensor_scalar(c2_t[:], cs1_t[:], -8.0, None, ALU.mult),
              reads=("cs1",), writes=("c2",))
    prog.comp("vector", lambda e: e.tensor_scalar(ch2_t[:], c2_t[:], 0.5, None, ALU.mult),
              reads=("c2",), writes=("ch2",))
    prog.comp("vector", lambda e: e.tensor_scalar(bg2_t[:], bg_t[:], 0.5, None, ALU.mult),
              reads=("bg",), writes=("bg2",))
    prog.comp("vector", lambda e: e.memset(eps_t[:], LN_EPS), writes=("eps",))

    CAST_LIST = [(w1_h, w1_s, D // P, DH), (w2_h, w2_s, DH // P, D),
                 (win_h, win_s, D // P, 2 * D), (wg_h, wg_s, D // P, 2 * D),
                 (wo_h, wo_s, D // P, D)]
    SCR_NAMES = ["scr_w1", "scr_w2", "scr_win", "scr_wg", "scr_wo"]

    def cast_weights(idx):
        scr_name = SCR_NAMES[idx]
        hsrc, sdst, nslab, cols = CAST_LIST[idx]
        for k0 in range(0, nslab, 4):
            k1 = min(k0 + 4, nslab)
            prog.dma("gpsimd", "qg",
                     lambda e, h=hsrc, sd=sdst, k0=k0, k1=k1: e.dma_start(
                         out=sd[k0 * P:k1 * P, :], in_=h[k0 * P:k1 * P, :]),
                     reads=(),
                     writes=tuple("%s:%d" % (scr_name, k) for k in range(k0, k1)))

    def grp_w1(h):
        src = w1_s.rearrange("(k p) c -> p k c", p=P)[:, :, h * 2048:(h + 1) * 2048]
        return src, 8, 2048, ["scr_w1:%d" % k for k in range(8)]

    def grp_w2(h):
        src = w2_s.rearrange("(k p) c -> p k c", p=P)[:, h * 16:(h + 1) * 16, :]
        return src, 16, 1024, ["scr_w2:%d" % k for k in range(h * 16, (h + 1) * 16)]

    def grp_win():
        return (win_s.rearrange("(k p) c -> p k c", p=P), 8, 2048,
                ["scr_win:%d" % k for k in range(8)])

    def grp_wg():
        return (wg_s.rearrange("(k p) c -> p k c", p=P), 8, 2048,
                ["scr_wg:%d" % k for k in range(8)])

    def grp_wo():
        return (wo_s.rearrange("(k p) c -> p k c", p=P), 8, 1024,
                ["scr_wo:%d" % k for k in range(8)])

    wf1 = w1_h.rearrange("(k p) c -> p k c", p=P)
    wf2 = w2_h.rearrange("(k p) c -> p k c", p=P)
    DIRECT_F32 = {
        0: (wf1[:, :, 0:2048], 8, 2048),
        1: (wf2[:, 0:16, :], 16, 1024),
        2: (wf1[:, :, 2048:4096], 8, 2048),
        3: (wf2[:, 16:32, :], 16, 1024),
        4: (win_h.rearrange("(k p) c -> p k c", p=P), 8, 2048),
        5: (wg_h.rearrange("(k p) c -> p k c", p=P), 8, 2048),
        6: (wo_h.rearrange("(k p) c -> p k c", p=P), 8, 1024),
    }

    def load_group(gidx, grp):
        j = gidx % 3
        if gidx in DIRECT_F32:
            src, nk, cols = DIRECT_F32.pop(gidx)
            dst = slots[j][:, 0:nk * cols].rearrange("p (k c) -> p k c", c=cols)
            prog.dma("gpsimd", "qg", lambda e, d=dst, s=src: e.dma_start(out=d, in_=s),
                     reads=(), writes=("slot%d" % j,))
            return
        src, nk, cols, rds = grp
        dst = slots[j][:, 0:nk * cols].rearrange("p (k c) -> p k c", c=cols)
        prog.dma("sync", "qs", lambda e, d=dst, s=src: e.dma_start(out=d, in_=s),
                 reads=tuple(rds), writes=("slot%d" % j,))

    def lhs(gidx, nk_cols, k, m):
        j = gidx % 3
        cols = nk_cols
        return slots[j][:, k * cols + m * P: k * cols + (m + 1) * P]

    def x_load(m, s):
        seq, t0 = m // 2, (m % 2) * TM
        src = x_h[seq].rearrange("(c p) t -> p c t", p=P)[:, :, t0 + s * TS: t0 + (s + 1) * TS]
        prog.dma("sync", "qs",
                 lambda e, s_=src, d=x_t[:, :, s * TS:(s + 1) * TS]: e.dma_start(out=d, in_=s_),
                 reads=(), writes=("x:%d" % s,))

    def x_store(m, s):
        seq, t0 = m // 2, (m % 2) * TM
        dst = out_h[seq].rearrange("(c p) t -> p c t", p=P)[:, :, t0 + s * TS: t0 + (s + 1) * TS]
        prog.dma("sync", "qs",
                 lambda e, d=dst, s_=x_t[:, :, s * TS:(s + 1) * TS]: e.dma_start(out=d, in_=s_),
                 reads=("x:%d" % s,), writes=("xout:%d:%d" % (m, s),))

    def layer_norm(s):
        off = s * TS
        for c in range(KC):
            xs = x_t[:, c, off:off + TS]
            prog.comp("vector", lambda e, a=xb_t[:, c, :], b=xs: e.tensor_copy(a, b),
                      reads=("x:%d" % s,), writes=("xb:%d" % c,))
            prog.comp("vector",
                      lambda e, a=x2b_t[:, c % 2, :], b=xb_t[:, c, :]: e.tensor_tensor(a, b, b, ALU.mult),
                      reads=("xb:%d" % c,), writes=("x2b%d" % (c % 2),))
            prog.comp("tensor",
                      lambda e, b=xb_t[:, c, :], st=(c == 0), sp=(c == KC - 1):
                      e.matmul(ps[6][:], ones_t[:], b, start=st, stop=sp,
                               skip_group_check=True),
                      reads=("ones", "xb:%d" % c), writes=("ps6",))
            prog.comp("tensor",
                      lambda e, b=x2b_t[:, c % 2, :], st=(c == 0), sp=(c == KC - 1):
                      e.matmul(ps[7][:], ones_t[:], b, start=st, stop=sp,
                               skip_group_check=True),
                      reads=("ones", "x2b%d" % (c % 2)), writes=("ps7",))
        inv = 1.0 / D
        prog.comp("scalar", lambda e: e.activation(ex2_t[:], ps[7][:], AF.Relu,
                                                   scale=inv, bias=eps_t[:, 0:1]),
                  reads=("ps7", "eps"), writes=("ex2",))
        prog.comp("scalar", lambda e: e.activation(meanf_t[:], ps[6][:], AF.Square, scale=inv),
                  reads=("ps6",), writes=("meanf",))
        prog.comp("scalar", lambda e: e.activation(meanb_t[:], ps[6][:], AF.Copy, scale=inv),
                  reads=("ps6",), writes=("meanb",))
        prog.comp("vector", lambda e: e.tensor_tensor(ex2_t[:], ex2_t[:], meanf_t[:], ALU.subtract),
                  reads=("ex2", "meanf"), writes=("ex2",))
        prog.comp("scalar", lambda e: e.activation(ex2_t[:], ex2_t[:], AF.Ln),
                  reads=("ex2",), writes=("ex2",))
        prog.comp("scalar", lambda e: e.activation(rstdb_t[:], ex2_t[:], AF.Exp, scale=-0.5),
                  reads=("ex2",), writes=("rstdb",))
        for c in range(KC):
            prog.comp("vector", lambda e, c=c: e.tensor_tensor(tb_t[:], xb_t[:, c, :], meanb_t[:], ALU.subtract),
                      reads=("xb:%d" % c, "meanb"), writes=("tb",))
            prog.comp("vector",
                      lambda e, c=c, o=off: e.tensor_tensor(xn_t[:, c, o:o + TS], tb_t[:], rstdb_t[:], ALU.mult),
                      reads=("tb", "rstdb"), writes=("xn:%d:%d" % (c, s),))

    def ffn_half(m, which, h, g1, g2, hook=None):
        for s in range(NSUB):
            off = s * TS
            for mm in range(16):
                bank = mm % 4
                for k in range(KC):
                    prog.comp("tensor",
                              lambda e, l=lhs(g1, 2048, k, mm), r=xn_t[:, k, off:off + TS],
                              st=(k == 0), sp=(k == KC - 1), b=bank:
                              e.matmul(ps[b][:], l, r, start=st, stop=sp),
                              reads=("slot%d" % (g1 % 3), "xn:%d:%d" % (k, s)),
                              writes=("ps%d" % bank,))
                bidx = h * 16 + mm
                prog.comp("scalar",
                          lambda e, b=bank, mm=mm, bi=bidx:
                          e.activation(hb_t[:, mm, :], ps[b][:], AF.Relu, bias=b1_t[:, bi:bi + 1]),
                          reads=("ps%d" % bank, "b1"), writes=("hb:%d" % mm,))
            for mo in range(KC):
                bank = 4 + mo % 4
                for k in range(16):
                    prog.comp("tensor",
                              lambda e, l=lhs(g2, 1024, k, mo), r=hb_t[:, k, :],
                              st=(k == 0), sp=(k == 15), b=bank:
                              e.matmul(ps[b][:], l, r, start=st, stop=sp),
                              reads=("slot%d" % (g2 % 3), "hb:%d" % k),
                              writes=("ps%d" % bank,))
                xs = x_t[:, mo, off:off + TS]
                prog.comp("vector",
                          lambda e, b=bank, xs=xs:
                          e.scalar_tensor_tensor(xs, ps[b][:], 0.5, xs, ALU.mult, ALU.add),
                          reads=("ps%d" % bank, "x:%d" % s), writes=("x:%d" % s,))
                if h == 0:
                    prog.comp("vector",
                              lambda e, xs=xs, mo=mo:
                              e.tensor_scalar(xs, xs, b2h_t[:, mo:mo + 1], None, ALU.add),
                              reads=("x:%d" % s, "b2h"), writes=("x:%d" % s,))
            if hook is not None:
                hook(s)

    def hawk_h1(m, g_in, skip_ln=False):
        prog.fence(HBKEYS)
        seq_start = (m % 2 == 0)
        if seq_start:
            prog.comp("vector", lambda e: e.memset(xatail_t[:], 0.0),
                      writes=tuple("xtl:%d" % c for c in range(KC)))
            prog.comp("vector", lambda e: e.memset(hc_t[:], 0.0),
                      writes=tuple("hc:%d" % c for c in range(KC)))
        if not skip_ln:
            for s in range(NSUB):
                layer_norm(s)
        for s in range(NSUB):
            off = s * TS
            for c in range(KC):
                prog.comp("vector",
                          lambda e, c=c: e.tensor_copy(xa_t[:, c, 0:4], xatail_t[:, c, :]),
                          reads=("xtl:%d" % c,), writes=("xat:%d" % c,))
            for mm in range(16):
                bank = mm % 4
                for k in range(KC):
                    prog.comp("tensor",
                              lambda e, l=lhs(g_in, 2048, k, mm), r=xn_t[:, k, off:off + TS],
                              st=(k == 0), sp=(k == KC - 1), b=bank:
                              e.matmul(ps[b][:], l, r, start=st, stop=sp),
                              reads=("slot%d" % (g_in % 3), "xn:%d:%d" % (k, s)),
                              writes=("ps%d" % bank,))
                if mm < KC:
                    c = mm
                    prog.comp("scalar",
                              lambda e, b=bank, c=c, o=off:
                              e.activation(gg_t[:, c, o:o + TS], ps[b][:], AF.Gelu),
                              reads=("ps%d" % bank,), writes=("gg:%d:%d" % (c, s),))
                else:
                    c = mm - KC
                    prog.comp("scalar",
                              lambda e, b=bank, c=c: e.activation(xa_t[:, c, 4:4 + TS], ps[b][:], AF.Copy),
                              reads=("ps%d" % bank,), writes=("xa:%d" % c,))
                    ceng = "vector"
                    xcs = xc_t[:, c, off:off + TS]
                    prog.comp(ceng,
                              lambda e, c=c, xcs=xcs: e.tensor_scalar(
                                  xcs, xa_t[:, c, 1:1 + TS], wcf_t[:, c * 4:c * 4 + 1],
                                  bconv_t[:, c:c + 1], ALU.mult, ALU.add),
                              reads=("xa:%d" % c, "xat:%d" % c, "wcf", "bconv"),
                              writes=("xc:%d:%d" % (c, s),))
                    for kk in range(1, 4):
                        prog.comp(ceng,
                                  lambda e, c=c, kk=kk, xcs=xcs: e.scalar_tensor_tensor(
                                      xcs, xa_t[:, c, 1 + kk:1 + kk + TS],
                                      wcf_t[:, c * 4 + kk:c * 4 + kk + 1], xcs, ALU.mult, ALU.add),
                                  reads=("xa:%d" % c, "xat:%d" % c, "wcf", "xc:%d:%d" % (c, s)),
                                  writes=("xc:%d:%d" % (c, s),))
                    prog.comp(ceng,
                              lambda e, c=c: e.tensor_copy(xatail_t[:, c, :], xa_t[:, c, TS:TS + 4]),
                              reads=("xa:%d" % c,), writes=("xtl:%d" % c,))

    def h2_g_block(s, g_g):
        off = s * TS
        for c in range(KC):
            bankF, bankI = c % 2, 2 + c % 2
            for k in range(KC):
                prog.comp("tensor",
                          lambda e, l=lhs(g_g, 2048, k, c), r=xc_t[:, k, off:off + TS],
                          st=(k == 0), sp=(k == KC - 1), b=bankF:
                          e.matmul(ps[b][:], l, r, start=st, stop=sp),
                          reads=("slot%d" % (g_g % 3), "xc:%d:%d" % (k, s)),
                          writes=("ps%d" % bankF,))
            for k in range(KC):
                prog.comp("tensor",
                          lambda e, l=lhs(g_g, 2048, k, KC + c), r=xc_t[:, k, off:off + TS],
                          st=(k == 0), sp=(k == KC - 1), b=bankI:
                          e.matmul(ps[b][:], l, r, start=st, stop=sp),
                          reads=("slot%d" % (g_g % 3), "xc:%d:%d" % (k, s)),
                          writes=("ps%d" % bankI,))
            prog.comp("scalar",
                      lambda e, b=bankF, c=c: e.activation(
                          sf_t[:, c, :], ps[b][:], AF.Tanh, scale=0.5,
                          bias=bg2_t[:, c:c + 1]),
                      reads=("ps%d" % bankF, "bg2"), writes=("sf:%d" % c,))
            prog.comp("scalar",
                      lambda e, b=bankI, c=c: e.activation(
                          si_t[:, c, :], ps[b][:], AF.Tanh, scale=0.5,
                          bias=bg2_t[:, KC + c:KC + c + 1]),
                      reads=("ps%d" % bankI, "bg2"), writes=("si:%d" % c,))
            prog.comp("scalar",
                      lambda e, c=c: e.activation(
                          sf_t[:, c, :], sf_t[:, c, :], AF.Exp,
                          scale=ch2_t[:, c:c + 1], bias=ch2_t[:, c:c + 1]),
                      reads=("sf:%d" % c, "ch2"), writes=("sf:%d" % c,))

    def h2_chains(s):
        off = s * TS

        def tail(c):
            pp = c % 2
            prog.comp("vector",
                      lambda e, c=c, pp=pp: e.tensor_tensor_scan(
                          hh_t[:, pp, :], sf_t[:, c, :], u_t[:, pp, :], hc_t[:, c:c + 1], ALU.mult, ALU.add),
                      reads=("sf:%d" % c, "u%d" % pp, "hc:%d" % c), writes=("hh%d" % pp,))
            prog.comp("gpsimd",
                      lambda e, c=c, pp=pp: e.tensor_copy(hc_t[:, c:c + 1], hh_t[:, pp, TS - 1:TS]),
                      reads=("hh%d" % pp,), writes=("hc:%d" % c,))
            prog.comp("gpsimd",
                      lambda e, c=c, pp=pp, o=off: e.tensor_tensor(
                          y_t[:, c, :], gg_t[:, c, o:o + TS], hh_t[:, pp, :], ALU.mult),
                      reads=("gg:%d:%d" % (c, s), "hh%d" % pp), writes=("y:%d" % c,))

        for c in range(KC):
            pp = c % 2
            prog.comp("scalar",
                      lambda e, c=c, pp=pp: e.activation(
                          q_t[:, pp, :], sf_t[:, c, :], AF.Square),
                      reads=("sf:%d" % c,), writes=("q%d" % pp,))
            prog.comp("vector",
                      lambda e, pp=pp: e.tensor_scalar(q_t[:, pp, :], q_t[:, pp, :], -0.25, 0.25000025, ALU.mult, ALU.add),
                      reads=("q%d" % pp,), writes=("q%d" % pp,))
            prog.comp("scalar",
                      lambda e, pp=pp: e.activation(q_t[:, pp, :], q_t[:, pp, :], AF.Sqrt),
                      reads=("q%d" % pp,), writes=("q%d" % pp,))
            prog.comp("vector",
                      lambda e, c=c, pp=pp, o=off: e.scalar_tensor_tensor(
                          u_t[:, pp, :], si_t[:, c, :], 1.0, xc_t[:, c, o:o + TS], ALU.add, ALU.mult),
                      reads=("si:%d" % c, "xc:%d:%d" % (c, s)), writes=("u%d" % pp,))
            prog.comp("gpsimd",
                      lambda e, pp=pp: e.tensor_tensor(u_t[:, pp, :], u_t[:, pp, :], q_t[:, pp, :], ALU.mult),
                      reads=("u%d" % pp, "q%d" % pp), writes=("u%d" % pp,))
            if c >= 1:
                tail(c - 1)
        tail(KC - 1)

    def h2_wout(s, g_o):
        off = s * TS
        for mo in range(KC):
            bank = 4 + mo % 4
            for k in range(KC):
                prog.comp("tensor",
                          lambda e, l=lhs(g_o, 1024, k, mo), r=y_t[:, k, :],
                          st=(k == 0), sp=(k == KC - 1), b=bank:
                          e.matmul(ps[b][:], l, r, start=st, stop=sp),
                          reads=("slot%d" % (g_o % 3), "y:%d" % k),
                          writes=("ps%d" % bank,))
            xs = x_t[:, mo, off:off + TS]
            prog.comp("vector",
                      lambda e, b=bank, xs=xs: e.tensor_tensor(xs, xs, ps[b][:], ALU.add),
                      reads=("ps%d" % bank, "x:%d" % s), writes=("x:%d" % s,))

    def hawk_h2(m, g_g, g_o):
        prog.fence(XNKEYS + LNKEYS + XAKEYS)
        h2_g_block(0, g_g)
        h2_chains(0)
        h2_g_block(1, g_g)
        h2_wout(0, g_o)
        h2_chains(1)
        h2_wout(1, g_o)

    gidx = 0

    def next_gidx():
        nonlocal gidx
        g = gidx
        gidx += 1
        return g

    for s in range(NSUB):
        x_load(0, s)
    g_next = [next_gidx(), next_gidx()]
    load_group(g_next[0], grp_w1(0))
    load_group(g_next[1], grp_w2(0))

    nmacro = DEBUG_NMACRO or NMACRO
    full = DEBUG_PHASES == "all"
    ln1_done = False
    for m in range(nmacro):
        g_w1a, g_w2a = g_next
        g_w1b, g_w2b = next_gidx(), next_gidx()
        load_group(g_w1b, grp_w1(1))
        if m > 0:
            prog.fence(GGKEYS + H2KEYS + YKEYS)
        if not ln1_done:
            layer_norm(0)
            layer_norm(1)
            f1a_hook = None
        else:
            f1a_hook = lambda s: layer_norm(1) if s == 0 else None
        ffn_half(m, 1, 0, g_w1a, g_w2a, hook=f1a_hook)
        load_group(g_w2b, grp_w2(1))
        g_in = next_gidx()
        load_group(g_in, grp_win())
        ffn_half(m, 1, 1, g_w1b, g_w2b,
                 hook=(layer_norm if full else None))
        g_g, g_o = next_gidx(), next_gidx()
        load_group(g_g, grp_wg())
        load_group(g_o, grp_wo())
        if m == 0:
            for ci in range(5):
                cast_weights(ci)
        if DEBUG_PHASES in ("f1+hawk", "all"):
            hawk_h1(m, g_in, skip_ln=full)
        g_w1a2, g_w2a2 = next_gidx(), next_gidx()
        load_group(g_w1a2, grp_w1(0))
        if DEBUG_PHASES in ("f1+hawk", "all"):
            hawk_h2(m, g_g, g_o)
        load_group(g_w2a2, grp_w2(0))
        g_w1b2, g_w2b2 = next_gidx(), next_gidx()
        load_group(g_w1b2, grp_w1(1))
        if full:
            prog.fence(GGKEYS + H2KEYS + YKEYS)
            layer_norm(0)
            layer_norm(1)
            ffn_half(m, 2, 0, g_w1a2, g_w2a2)
        load_group(g_w2b2, grp_w2(1))
        if m < nmacro - 1:
            g_next = [next_gidx(), next_gidx()]
            load_group(g_next[0], grp_w1(0))

        def f2b_hook(s, m=m):
            x_store(m, s)
            if m < nmacro - 1:
                x_load(m + 1, s)

        if full:
            ffn_half(m, 2, 1, g_w1b2, g_w2b2, hook=f2b_hook)
            if m < nmacro - 1:
                layer_norm(0)
            ln1_done = m < nmacro - 1
        else:
            prog.fence(HBKEYS)
            for s in range(NSUB):
                x_store(m, s)
                if m < nmacro - 1:
                    x_load(m + 1, s)
        if m < nmacro - 1:
            load_group(g_next[1], grp_w2(0))

    prog.final_wait("sync", ["xout:%d:%d" % (m, s)
                             for m in range(NMACRO) for s in range(NSUB)])

    with nc.Block() as block:
        prog.emit(block)
    return nc


_NC_CACHE = None


def _get_nc():
    global _NC_CACHE
    if _NC_CACHE is None:
        _NC_CACHE = build_nc()
    return _NC_CACHE


def kernel(L0_attn_x, L0_attn_w_in, L0_attn_w_conv, L0_attn_b_conv,
           L0_attn_w_gates, L0_attn_b_gates, L0_attn_forget_base,
           L0_attn_w_out, L0_ffn_w1, L0_ffn_b1, L0_ffn_w2, L0_ffn_b2):
    x = np.asarray(L0_attn_x, dtype=np.float32)
    assert x.shape == (32, T, D)

    def vecP(v, n):
        return np.ascontiguousarray(np.asarray(v, np.float32).reshape(n, P).T)

    common = {
        "w1": np.ascontiguousarray(np.asarray(L0_ffn_w1, np.float32)),
        "w2": np.ascontiguousarray(np.asarray(L0_ffn_w2, np.float32)),
        "winT": np.ascontiguousarray(np.asarray(L0_attn_w_in, np.float32).T),
        "wgT": np.ascontiguousarray(np.asarray(L0_attn_w_gates, np.float32).T),
        "woT": np.ascontiguousarray(np.asarray(L0_attn_w_out, np.float32).T),
        "bconv": vecP(L0_attn_b_conv, KC),
        "fb": vecP(L0_attn_forget_base, KC),
        "bg": vecP(L0_attn_b_gates, 2 * KC),
        "b1": vecP(L0_ffn_b1, DH // P),
        "b2": vecP(L0_ffn_b2, KC),
        "wcf": np.ascontiguousarray(
            np.asarray(L0_attn_w_conv, np.float32)[:, 0, :]
            .reshape(KC, P, 4).transpose(1, 0, 2).reshape(P, KC * 4)),
    }
    in_maps = []
    for core in range(8):
        shard = x[core * SEQ_PER_CORE:(core + 1) * SEQ_PER_CORE]
        xs = np.ascontiguousarray(shard.transpose(0, 2, 1))
        im = dict(common)
        im["x"] = xs
        in_maps.append(im)

    nc = _get_nc()
    res = run_bass_kernel_spmd(nc, in_maps, core_ids=list(range(8)))
    outs = []
    for core in range(8):
        o = np.asarray(res.results[core]["xout"])
        outs.append(o.transpose(0, 2, 1))
    return np.ascontiguousarray(np.concatenate(outs, axis=0)).astype(np.float32)
